# version 01
# baseline (speedup 1.0000x reference)
import numpy as np
import ml_dtypes
import concourse.bass as bass
import concourse.mybir as mybir
from concourse.bass import IndirectOffsetOnAxis
from concourse.bass_utils import run_bass_kernel_spmd

F32 = mybir.dt.float32
BF16 = mybir.dt.bfloat16
U32 = mybir.dt.uint32
AF = mybir.ActivationFunctionType
ALU = mybir.AluOpType
AX = mybir.AxisListType
BF = ml_dtypes.bfloat16

N_CORES = 8
D = 4096
T = 2048
HF = 11008
NF4_CODE = np.array([-1.0, -0.6961928009986877, -0.5250730514526367,
                     -0.39491748809814453, -0.28444138169288635, -0.18477343022823334,
                     -0.09105003625154495, 0.0, 0.07958029955625534, 0.16093020141124725,
                     0.24611230194568634, 0.33791524171829224, 0.44070982933044434,
                     0.5626170039176941, 0.7229568362236023, 1.0], dtype=np.float32)
NF4_BOUNDS = ((NF4_CODE[:-1] + NF4_CODE[1:]) * np.float32(0.5)).astype(np.float32)
NF4_DELTA = (NF4_CODE[1:] - NF4_CODE[:-1]).astype(np.float32)

NH = 11


class Prog:
    ENGS = ("sync", "vector", "scalar", "gpsimd", "tensor")

    def __init__(self, nc):
        self.nc = nc
        self.ops = {k: [] for k in self.ENGS}
        self._sem_guards = []
        self.sems = {}
        self.cnt = {}
        self._sb_guards = []
        self._cvals = []
        self.s_cst = self.sem("s_cst")
        self._cst = self.sb("cstbuf", [128, 32], F32)

    def constf(self, val):
        val = float(val)
        if val in self._cvals:
            i = self._cvals.index(val)
        else:
            self._cvals.append(val)
            i = len(self._cvals) - 1
        return self._cst[:, i:i + 1]

    def cw(self):
        return (self.s_cst, lambda: len(self._cvals))

    def sem(self, name):
        g = self.nc.semaphore(name)
        self.sems[name] = g.__enter__()
        self._sem_guards.append(g)
        self.cnt[name] = 0
        return name

    def sb(self, name, shape, dt):
        g = self.nc.sbuf_tensor(name, shape, dt)
        h = g.__enter__()
        self._sb_guards.append((name, g))
        return h

    def sbfree(self, *names):
        for name in names:
            nm, g = self._sb_guards.pop()
            assert nm == name, f"free order: expected {nm}, got {name}"
            g.__exit__(None, None, None)

    def ps(self, name, shape, dt):
        g = self.nc.psum_tensor(name, shape, dt)
        h = g.__enter__()
        self._ps_guards = getattr(self, "_ps_guards", [])
        self._ps_guards.append(g)
        return h

    def bump(self, sem, n=1):
        self.cnt[sem] += n
        return self.cnt[sem]

    def emit(self, eng, fn=None, waits=(), inc=None, incn=1):
        if inc:
            self.bump(inc, incn)
        self.ops[eng].append((fn, tuple(waits), inc, incn))

    def build(self):
        nc = self.nc
        sems = self.sems
        pro = []
        for i, v in enumerate(self._cvals):
            pro.append(((lambda e, i=i, v=v: e.memset(self._cst[:, i:i + 1], v)),
                        (), self.s_cst, 1))
        self.ops["vector"] = pro + self.ops["vector"]
        with nc.Block() as block:
            def mk(eng_name):
                def body(e):
                    for fn, waits, inc, incn in self.ops[eng_name]:
                        for s, v in waits:
                            if callable(v):
                                v = v()
                            if v > 0:
                                e.wait_ge(sems[s], v)
                        if fn is not None:
                            inst = fn(e)
                            if inc is not None:
                                inst.then_inc(sems[inc], incn)
                return body
            for name in self.ENGS:
                if self.ops[name]:
                    getattr(block, name)(mk(name))
        return nc



def build_prep(nh, ncast):
    nc = bass.Bass("TRN2", target_bir_lowering=False)
    w1s = nc.dram_tensor("w1s", [nh * 128, 4096], F32, kind="ExternalInput").ap()
    w2s = nc.dram_tensor("w2s", [nh * 128, 4096], F32, kind="ExternalInput").ap()
    w3s = nc.dram_tensor("w3s", [512, 11008], F32, kind="ExternalInput").ap()
    iden = nc.dram_tensor("iden", [128, 128], F32, kind="ExternalInput").ap()
    csrc = nc.dram_tensor("castsrc", [ncast], F32, kind="ExternalInput").ap()
    w1t = nc.dram_tensor("w1t", [4096, nh * 128], BF16, kind="ExternalOutput").ap()
    w2t = nc.dram_tensor("w2t", [4096, nh * 128], BF16, kind="ExternalOutput").ap()
    w3t = nc.dram_tensor("w3t", [11008, 512], BF16, kind="ExternalOutput").ap()
    cdst = nc.dram_tensor("castdst", [ncast], BF16, kind="ExternalOutput").ap()

    jobs = []
    import os as _os
    for ht in range(nh):
        for dh in range(2):
            jobs.append((w1s, w1t, ht * 128, dh * 2048, 2048, ht * 128))
            jobs.append((w2s, w2t, ht * 128, dh * 2048, 2048, ht * 128))
    for rt in range(4):
        for seg in range(6):
            c0 = seg * 2048
            jobs.append((w3s, w3t, rt * 128, c0, min(2048, 11008 - c0), rt * 128))
    import os
    _mj = os.environ.get("PREP_MAXJOBS")
    if _mj:
        jobs = jobs[:int(_mj)]

    P = Prog(nc)
    wbuf = P.sb("wbuf", [128, 2, 2048], F32)
    am = P.sb("am", [128, 2, 32], F32)
    cmpb = P.sb("cmpb", [128, 2048], BF16)
    acc = P.sb("acc", [128, 2048], F32)
    dq = P.sb("dq", [128, 2, 2048], BF16)
    ev = P.sb("ev", [128, 2, 16, 128], BF16)

    s_wld = [P.sem("s_wld0"), P.sem("s_wld1")]
    s_dq = P.sem("s_dq")
    s_tpd = [P.sem("s_tpd0"), P.sem("s_tpd1")]
    s_st = [P.sem("s_st0"), P.sem("s_st1")]
    s_cast = P.sem("s_cast")

    csz = ncast // 4
    for i in range(4):
        P.emit("gpsimd",
               lambda e, i=i: e.dma_start(out=cdst[i * csz:(i + 1) * csz],
                                          in_=csrc[i * csz:(i + 1) * csz]),
               inc=s_cast, incn=16)

    nload = [0, 0]
    tcum = [0, 0]
    nstb = [0, 0]
    lastread_byb = [0, 0]

    for i, (src_, dst, r0, c0, w, oc0) in enumerate(jobs):
        b = i % 2
        nb = w // 64
        ntpj = w // 128
        nload[b] += 1
        P.emit("sync",
               lambda e, src_=src_, r0=r0, c0=c0, w=w, b=b:
               e.dma_start(out=wbuf[:, b, 0:w], in_=src_[r0:r0 + 128, c0:c0 + w]),
               waits=[(s_dq, lastread_byb[b])], inc=s_wld[b], incn=16)
        P.emit("vector", lambda e, b=b, w=w, nb=nb:
               e.tensor_reduce(out=am[:, b, 0:nb],
                               in_=wbuf[:, b, 0:w].rearrange("p (b e) -> p b e", e=64),
                               axis=AX.X, op=ALU.max, apply_absolute_value=True),
               waits=[(s_wld[b], 16 * nload[b]), (s_dq, lastread_byb[b])], inc=s_dq)
        P.emit("vector", lambda e, w=w:
               e.memset(acc[:, 0:w], float(NF4_CODE[0])),
               waits=[(s_dq, P.cnt[s_dq])], inc=s_dq)
        for jb in range(15):
            P.emit("vector", lambda e, b=b, w=w, nb=nb, jb=jb:
                   e.scalar_tensor_tensor(
                       out=cmpb[:, 0:w].rearrange("p (b e) -> p b e", e=64),
                       in0=am[:, b, 0:nb].unsqueeze(2).to_broadcast([128, nb, 64]),
                       scalar=float(NF4_BOUNDS[jb]),
                       in1=wbuf[:, b, 0:w].rearrange("p (b e) -> p b e", e=64),
                       op0=ALU.mult, op1=ALU.is_lt),
                   waits=[(s_dq, P.cnt[s_dq])], inc=s_dq)
            P.emit("vector", lambda e, w=w, jb=jb:
                   e.scalar_tensor_tensor(out=acc[:, 0:w], in0=cmpb[:, 0:w],
                                          scalar=float(NF4_DELTA[jb]), in1=acc[:, 0:w],
                                          op0=ALU.mult, op1=ALU.add),
                   waits=[(s_dq, P.cnt[s_dq])], inc=s_dq)
        dqw = [(s_dq, P.cnt[s_dq]), (s_tpd[b], 16 * tcum[b])]
        P.emit("vector", lambda e, b=b, w=w, nb=nb:
               e.tensor_tensor(out=dq[:, b, 0:w].rearrange("p (b e) -> p b e", e=64),
                               in0=acc[:, 0:w].rearrange("p (b e) -> p b e", e=64),
                               in1=am[:, b, 0:nb].unsqueeze(2).to_broadcast([128, nb, 64]),
                               op=ALU.mult),
               waits=dqw, inc=s_dq)
        lastread_byb[b] = P.cnt[s_dq]
        dq_done = P.cnt[s_dq]
        for k in range(ntpj):
            tw = [(s_dq, dq_done)] if k == 0 else []
            if k == 0 and nstb[b]:
                tw.append((s_st[b], 16 * nstb[b]))
            P.emit("scalar", lambda e, b=b, k=k:
                   e.dma_start(out=ev[:, b, k, :], in_=dq[:, b, k * 128:(k + 1) * 128],
                               transpose=True),
                   waits=tw, inc=s_tpd[b], incn=16)
        tcum[b] += ntpj
        for k in range(ntpj):
            sw = [(s_tpd[b], P.cnt[s_tpd[b]])] if k == 0 else []
            P.emit("scalar", lambda e, b=b, k=k, dst=dst, c0=c0, oc0=oc0:
                   e.dma_start(out=dst[c0 + k * 128:c0 + (k + 1) * 128, oc0:oc0 + 128],
                               in_=ev[:, b, k, :]),
                   waits=sw, inc=s_st[b], incn=16)
        nstb[b] += ntpj
    P.emit("sync", None, waits=[(s_st[0], 16 * nstb[0]), (s_st[1], 16 * nstb[1]),
                                (s_cast, 64)])
    return P.build()


def _dft_tables():
    L = 4096
    NF = 2049
    f = np.arange(NF, dtype=np.float64)
    t = np.arange(L, dtype=np.float64)
    ang = 2.0 * np.pi * np.outer(t, f) / L
    C = np.zeros((L, 2176), np.float32); C[:, :NF] = np.cos(ang)
    S = np.zeros((L, 2176), np.float32); S[:, :NF] = -np.sin(ang)
    w = np.full(NF, 2.0); w[0] = 1.0; w[2048] = 1.0
    l2 = np.arange(2049, dtype=np.float64)
    ang2 = 2.0 * np.pi * np.outer(f, l2) / L
    C2 = np.zeros((2176, 2112), np.float32)
    C2[:NF, :2049] = ((w[:, None] / L) * np.cos(ang2)).astype(np.float32)
    return C, S, C2


def _prep_host(inputs):
    w1 = np.asarray(inputs["L0_ffn_w1"]); w2 = np.asarray(inputs["L0_ffn_w2"])
    w3 = np.asarray(inputs["L0_ffn_w3"])
    wqT = np.ascontiguousarray(np.asarray(inputs["L1_attn_Wq"]).T)
    wkT = np.ascontiguousarray(np.asarray(inputs["L1_attn_Wk"]).T)
    wvT = np.ascontiguousarray(np.asarray(inputs["L1_attn_Wv"]).T)
    woT = np.ascontiguousarray(np.asarray(inputs["L1_attn_Wo"]).T)
    AT = np.ascontiguousarray(np.asarray(inputs["L1_ffn_A"]).reshape(128, 4096).T)
    BsT = np.ascontiguousarray(np.asarray(inputs["L1_ffn_B"]).transpose(0, 2, 1).reshape(128, 4096))
    WrT = np.ascontiguousarray(np.asarray(inputs["L1_ffn_Wr"]).T)
    Cc, Ss, C2 = _dft_tables()
    cosF, sinF = _rope_tables()
    parts = [wqT, wkT, wvT, woT, AT, BsT, WrT, Cc, Ss, C2, cosF, sinF]
    shapes = [p.shape for p in parts]
    sizes = [p.size for p in parts]
    flat = np.concatenate([p.ravel() for p in parts]).astype(np.float32)
    quantum = 128 * 512 * N_CORES
    flat = np.pad(flat, (0, (-len(flat)) % quantum))
    ncast = len(flat) // N_CORES
    iden = np.eye(128, dtype=np.float32)
    w1p = np.zeros((NH * 128 * N_CORES, 4096), np.float32); w1p[:11008] = w1
    w2p = np.zeros((NH * 128 * N_CORES, 4096), np.float32); w2p[:11008] = w2
    nhr = NH * 128
    in_maps = []
    for c in range(N_CORES):
        in_maps.append({
            "w1s": np.ascontiguousarray(w1p[c * nhr:(c + 1) * nhr]),
            "w2s": np.ascontiguousarray(w2p[c * nhr:(c + 1) * nhr]),
            "w3s": np.ascontiguousarray(w3[c * 512:(c + 1) * 512]),
            "castsrc": flat[c * ncast:(c + 1) * ncast],
            "iden": iden,
        })
    return in_maps, ncast, shapes, sizes


CAST_NAMES = ["wqT", "wkT", "wvT", "woT", "AT", "BsT", "WrT", "C", "S", "C2",
              "cosF", "sinF"]


def _rope_tables():
    inv = 1.0 / (10000.0 ** (np.arange(0, 128, 2, dtype=np.float64) / 128))
    t = np.arange(T, dtype=np.float64)
    freqs = np.outer(t, inv)
    emb = np.concatenate([freqs, freqs], axis=-1)
    return (np.ascontiguousarray(np.cos(emb).T).astype(np.float32),
            np.ascontiguousarray(np.sin(emb).T).astype(np.float32))


def run_prep(inputs, launcher=run_bass_kernel_spmd):
    in_maps, ncast, shapes, sizes = _prep_host(inputs)
    nc = build_prep(NH, ncast)
    r = launcher(nc, in_maps, core_ids=list(range(N_CORES)))
    res_by_core = r.results
    out = {
        "w1t": np.concatenate([np.asarray(res_by_core[c]["w1t"]) for c in range(N_CORES)], axis=1)[:, :11008],
        "w2t": np.concatenate([np.asarray(res_by_core[c]["w2t"]) for c in range(N_CORES)], axis=1)[:, :11008],
        "w3t": np.concatenate([np.asarray(res_by_core[c]["w3t"]) for c in range(N_CORES)], axis=1),
    }
    cast = np.concatenate([np.asarray(res_by_core[c]["castdst"]) for c in range(N_CORES)])
    off = 0
    for n, shp, sz in zip(CAST_NAMES, shapes, sizes):
        out[n] = cast[off:off + sz].reshape(shp)
        off += sz
    return out






def build_l0a():
    nc = bass.Bass("TRN2", target_bir_lowering=False)
    x_d = nc.dram_tensor("x", [T, D], F32, kind="ExternalInput").ap()
    Cb_d = nc.dram_tensor("Cb", [D, 2176], BF16, kind="ExternalInput").ap()
    Sb_d = nc.dram_tensor("Sb", [D, 2176], BF16, kind="ExternalInput").ap()
    C2_d = nc.dram_tensor("C2b", [2176, 2112], BF16, kind="ExternalInput").ap()
    x1_d = nc.dram_tensor("x1", [T, D], F32, kind="ExternalOutput").ap()
    vv_d = nc.dram_tensor("vv", [T, 2 * D], BF16, kind="Internal").ap()

    P = Prog(nc)
    xb = P.sb("xb", [128, 2, D], BF16)
    vln = P.sb("vln", [128, 2, D], BF16)
    vfm = P.sb("vfm", [128, 32, 512], BF16)
    c2sl = P.sb("c2sl", [128, 2, 17, 512], BF16)
    csl = P.sb("csl", [128, 2, 32, 128], BF16)
    ssl = P.sb("ssl", [128, 2, 32, 128], BF16)
    pbuf = P.sb("pbuf", [128, 17, 512], BF16)
    sq1 = P.sb("sq1", [128, 2, 512], BF16)
    sq2 = P.sb("sq2", [128, 2, 512], BF16)
    cfull = P.sb("cfull", [128, 4096], BF16)
    roll = P.sb("roll", [128, 2, D], BF16)
    rollw = P.sb("rollw", [128, D], BF16)
    agg = P.sb("agg", [128, D], BF16)
    x1t = P.sb("x1t", [128, D], F32)
    t8 = P.sb("t8", [128, 8], BF16)
    t8f = P.sb("t8f", [128, 8], F32)
    i8 = P.sb("i8", [128, 8], U32)
    e8 = P.sb("e8", [128, 8], F32)
    w8 = P.sb("w8", [128, 8], F32)
    z8 = P.sb("z8", [128, 1], F32)
    rz = P.sb("rz", [128, 1], F32)
    nmx = P.sb("nmx", [128, 1], F32)
    ibase = P.sb("ibase", [128, 1], U32)
    ixk = P.sb("ixk", [128, 2, 1], U32)
    st = P.sb("st", [128, 12], F32)
    repsum = P.ps("repsum", [128, 1, 512], F32)
    impsum = P.ps("impsum", [128, 1, 512], F32)
    cop = [P.ps("cop0", [128, 512], F32), P.ps("cop1", [128, 512], F32),
           P.ps("cop2", [128, 512], F32), P.ps("cop3", [128, 512], F32)]

    s_xl = [P.sem("s_xl0"), P.sem("s_xl1")]
    s_d = P.sem("s_d")
    s_a = P.sem("s_a")
    s_g = P.sem("s_g")
    s_vv = [P.sem("s_vv0"), P.sem("s_vv1")]
    s_tp = [P.sem("s_tp0"), P.sem("s_tp1")]
    s_cs = [P.sem("s_cs0"), P.sem("s_cs1")]
    s_c2 = [P.sem("s_c20"), P.sem("s_c21")]
    s_mm = P.sem("s_mm")
    s_rl = [P.sem("s_rl0"), P.sem("s_rl1")]
    s_xc = P.sem("s_xc")
    s_out = P.sem("s_out")


    LNW = 1.0 / 4096.0
    EPS = 1e-5
    LCS = [(0, 512), (512, 512), (1024, 512), (1536, 512), (1540, 512)]

    for g in range(16):
        P.emit("gpsimd", lambda e, g=g: e.dma_start(
            out=x1_d[g * 128:(g + 1) * 128, :], in_=x_d[g * 128:(g + 1) * 128, :]),
            inc=s_xc, incn=16)

    state = {"mm_after_dft": 0, "rollread": [0, 0], "glast": [0, 0], "selread_g": 0, "t9_acc": 0}

    def emit_ln_scalars(pref):
        P.emit("vector", lambda e: e.tensor_scalar(
            out=st[:, 2:3], in0=st[:, 0:1], scalar1=LNW, scalar2=None, op0=ALU.mult),
            waits=[(s_d, P.cnt[s_d])], inc=s_d)
        P.emit("vector", lambda e: e.tensor_scalar(
            out=st[:, 3:4], in0=st[:, 1:2], scalar1=LNW, scalar2=EPS,
            op0=ALU.mult, op1=ALU.add),
            waits=[(s_a, P.cnt[s_a]), (s_d, P.cnt[s_d])], inc=s_d)
        P.emit("vector", lambda e: e.tensor_tensor(
            out=st[:, 4:5], in0=st[:, 2:3], in1=st[:, 2:3], op=ALU.mult),
            waits=[(s_d, P.cnt[s_d])], inc=s_d)
        P.emit("vector", lambda e: e.tensor_tensor(
            out=st[:, 5:6], in0=st[:, 3:4], in1=st[:, 4:5], op=ALU.subtract),
            waits=[(s_d, P.cnt[s_d])], inc=s_d)
        P.emit("scalar", lambda e: e.activation(
            out=st[:, 6:7], in_=st[:, 5:6], func=AF.Ln),
            waits=[(s_d, P.cnt[s_d]), (s_a, P.cnt[s_a])], inc=s_a)
        nhalf = P.constf(-0.5)
        P.emit("scalar", lambda e, nhalf=nhalf: e.activation(
            out=st[:, 7:8], in_=st[:, 6:7], func=AF.Exp, scale=nhalf),
            waits=[(s_a, P.cnt[s_a]), P.cw()], inc=s_a)
        P.emit("vector", lambda e: e.scalar_tensor_tensor(
            out=st[:, 8:9], in0=st[:, 2:3], scalar=-1.0, in1=st[:, 7:8],
            op0=ALU.mult, op1=ALU.mult),
            waits=[(s_a, P.cnt[s_a]), (s_d, P.cnt[s_d])], inc=s_d)

    def emit_ln(g):
        import os as _os
        b = g % 2
        tau = g // 4
        P.emit("gpsimd", lambda e, g=g, b=b: e.dma_start(
            out=xb[:, b, :], in_=x_d[g * 128:(g + 1) * 128, :]),
            waits=[(s_a, state.get("lnapply%d" % b, 0))], inc=s_xl[b], incn=16)
        P.emit("vector", lambda e, b=b: e.tensor_scalar(
            out=rollw[:], in0=xb[:, b, :], scalar1=0.0, scalar2=None,
            op0=ALU.add, op1=ALU.add, accum_out=st[:, 0:1]),
            waits=[(s_xl[b], 16 * (g // 2 + 1)), (s_d, P.cnt[s_d])], inc=s_d)
        P.emit("scalar", lambda e, b=b: e.activation(
            out=cfull[:], in_=xb[:, b, :], func=AF.Square, accum_out=st[:, 1:2]),
            waits=[(s_xl[b], 16 * (g // 2 + 1)), (s_a, P.cnt[s_a]),
                   (s_d, state["selread_g"])], inc=s_a)
        emit_ln_scalars("ln")
        vw = [(s_d, P.cnt[s_d]), (s_a, P.cnt[s_a])]
        if g >= 2:
            vw += [(s_tp[b], 16 * 32 * (g // 2)), (s_vv[b], 16 * 2 * (g // 2))]
        P.emit("scalar", lambda e, b=b: e.activation(
            out=vln[:, b, :], in_=xb[:, b, :], func=AF.Identity,
            scale=st[:, 7:8], bias=st[:, 8:9]),
            waits=vw, inc=s_a)
        state["lnapply%d" % b] = P.cnt[s_a]
        aval = P.cnt[s_a]
        for h in range(2):
            P.emit("gpsimd", lambda e, g=g, b=b, h=h: e.dma_start(
                out=vv_d[g * 128:(g + 1) * 128, h * D:(h + 1) * D], in_=vln[:, b, :]),
                waits=[(s_a, aval)] if h == 0 else [], inc=s_vv[b], incn=16)
        c = g % 4
        for k in range(32):
            tw = [(s_a, aval)] if k == 0 else []
            if k == 0 and tau >= 1:
                tw.append((s_mm, state["mm_after_dft"]))
            P.emit("scalar", lambda e, b=b, c=c, k=k: e.dma_start(
                out=vfm[:, k, c * 128:(c + 1) * 128],
                in_=vln[:, b, k * 128:(k + 1) * 128], transpose=True),
                waits=tw, inc=s_tp[b], incn=16)

    def emit_dft(tau):
        import os as _os
        if _os.environ.get("L0A_SKIPDFT"):
            P.emit("vector", lambda e: e.memset(pbuf[:], 1.0),
                   waits=[(s_d, P.cnt[s_d]),
                          (s_tp[0], 16 * 32 * 2 * (tau + 1)),
                          (s_tp[1], 16 * 32 * 2 * (tau + 1))], inc=s_d)
            state["mm_after_dft"] = P.cnt[s_mm]
            state["pbuf_done"] = P.cnt[s_d]
            return
        sq_done = [P.cnt[s_a]] * 2
        sq_read = state.get("sq_read", [0, 0])
        for ft in range(17):
            fb = ft % 2
            pb = 0
            P.emit("sync", lambda e, ft=ft, fb=fb: e.dma_start(
                out=csl[:, fb, :, :],
                in_=Cb_d.rearrange("(c p) f -> p c f", p=128)[:, :, ft * 128:(ft + 1) * 128]),
                waits=[(s_mm, P.cnt[s_mm])] if ft >= 2 else [], inc=s_cs[fb], incn=16)
            P.emit("sync", lambda e, ft=ft, fb=fb: e.dma_start(
                out=ssl[:, fb, :, :],
                in_=Sb_d.rearrange("(c p) f -> p c f", p=128)[:, :, ft * 128:(ft + 1) * 128]),
                inc=s_cs[fb], incn=16)
            nld = P.cnt[s_cs[fb]]
            mmw = [(s_cs[fb], nld), (s_a, sq_done[0] if ft > 0 else P.cnt[s_a])]
            if ft == 0:
                mmw += [(s_tp[0], 16 * 32 * 2 * (tau + 1)),
                        (s_tp[1], 16 * 32 * 2 * (tau + 1))]
            for dc in range(32):
                P.emit("tensor", lambda e, fb=fb, dc=dc: e.matmul(
                    out=repsum[:, 0, :], lhsT=csl[:, fb, dc, :], rhs=vfm[:, dc, :],
                    start=(dc == 0), stop=(dc == 31)),
                    waits=mmw if dc == 0 else [])
            for dc in range(32):
                w = []
                if dc == 31:
                    P.emit("tensor", lambda e, fb=fb, dc=dc: e.matmul(
                        out=impsum[:, 0, :], lhsT=ssl[:, fb, dc, :], rhs=vfm[:, dc, :],
                        start=(dc == 0), stop=True), inc=s_mm)
                else:
                    P.emit("tensor", lambda e, fb=fb, dc=dc: e.matmul(
                        out=impsum[:, 0, :], lhsT=ssl[:, fb, dc, :], rhs=vfm[:, dc, :],
                        start=(dc == 0), stop=False))
            mmv = P.cnt[s_mm]
            P.emit("scalar", lambda e, fb=fb: e.activation(
                out=sq1[:, fb, :], in_=repsum[:, 0, :], func=AF.Square),
                waits=[(s_mm, mmv), (s_a, P.cnt[s_a]), (s_d, sq_read[fb])], inc=s_a)
            P.emit("scalar", lambda e, fb=fb: e.activation(
                out=sq2[:, fb, :], in_=impsum[:, 0, :], func=AF.Square),
                waits=[(s_a, P.cnt[s_a])], inc=s_a)
            sq_done[0] = P.cnt[s_a]
            P.emit("vector", lambda e, ft=ft, fb=fb: e.tensor_tensor(
                out=pbuf[:, ft, :], in0=sq1[:, fb, :], in1=sq2[:, fb, :], op=ALU.add),
                waits=[(s_a, P.cnt[s_a]), (s_d, P.cnt[s_d])], inc=s_d)
            sq_read[fb] = P.cnt[s_d]
        state["sq_read"] = sq_read
        state["mm_after_dft"] = P.cnt[s_mm]
        state["pbuf_done"] = P.cnt[s_d]

    def emit_corr(tau, c):
        import os as _os
        g = tau * 4 + c
        b = g % 2
        evac_prev_a = P.cnt[s_a]
        evac_prev_d = P.cnt[s_d]
        if _os.environ.get("L0A_SKIPCORR"):
            P.emit("vector", lambda e: e.memset(cfull[:], 1.0),
                   waits=[(s_d, max(P.cnt[s_d], state["selread_g"])),
                          (s_a, P.cnt[s_a])], inc=s_d)
            mmv = P.cnt[s_mm]
        lcmm = state.get("lcmm", [0, 0])
        _nlc = int(_os.environ.get("L0A_NLC", "5"))
        LCS2 = LCS[:_nlc]
        if _nlc < 5:
            _fl = 2048 if _nlc == 4 else LCS[_nlc][0]
            P.emit("vector", lambda e, _fl=_fl: e.memset(cfull[:, _fl:4096], 1.0),
                   waits=[(s_d, max(P.cnt[s_d], state["selread_g"])), (s_a, P.cnt[s_a])], inc=s_d)
        for li, (l0, lw) in enumerate(LCS2):
            if _os.environ.get("L0A_SKIPCORR"):
                break
            lb = li % 2
            P.emit("sync", lambda e, li=li, lb=lb, l0=l0, lw=lw: e.dma_start(
                out=c2sl[:, lb, :, 0:lw],
                in_=C2_d.rearrange("(ft p) l -> p ft l", p=128)[:, :, l0:l0 + lw]),
                waits=[(s_mm, lcmm[lb])] if lcmm[lb] else [],
                inc=s_c2[lb], incn=16)
            mmw = [(s_c2[lb], P.cnt[s_c2[lb]]), (s_d, state["pbuf_done"]),
                   (s_a, evac_prev_a), (s_d, evac_prev_d)]
            for ft in range(17):
                P.emit("tensor", lambda e, c=c, ft=ft, lb=lb, li=li, lw=lw: e.matmul(
                    out=(cop[li] if li < 4 else repsum[:, 0, :])[:, 0:lw], lhsT=pbuf[:, ft, c * 128:(c + 1) * 128],
                    rhs=c2sl[:, lb, ft, 0:lw], start=(ft == 0), stop=(ft == 16)),
                    waits=mmw if ft == 0 else [], inc=s_mm if ft == 16 else None)
            lcmm[lb] = P.cnt[s_mm]
        state["lcmm"] = lcmm
        if not _os.environ.get("L0A_SKIPCORR"):
            mmv = P.cnt[s_mm]
            for li2, (l02, lw2) in enumerate(LCS2):
                P.emit("scalar", lambda e, li2=li2, l02=l02, lw2=lw2: e.activation(
                    out=cfull[:, l02:l02 + lw2], in_=(cop[li2] if li2 < 4 else repsum[:, 0, :])[:, 0:lw2], func=AF.Copy),
                    waits=[(s_mm, mmv), (s_a, P.cnt[s_a]), (s_d, state["selread_g"])]
                    if li2 == 0 else [(s_a, P.cnt[s_a])], inc=s_a)
            evdone = P.cnt[s_a]
            if _nlc == 5:
                for li2, (l02, lw2) in enumerate(LCS2[:4]):
                    lo = l02 if li2 > 0 else 1
                    n = l02 + lw2 - lo
                    P.emit("vector", lambda e, li2=li2, lo=lo, l02=l02, n=n: e.tensor_copy(
                        out=cfull[:, 4096 - lo:4096 - lo - n:-1],
                        in_=cop[li2][:, lo - l02:lo - l02 + n]),
                        waits=[(s_mm, mmv), (s_d, P.cnt[s_d]), (s_a, evdone)], inc=s_d)
        P.emit("vector", lambda e: e.max(out=t8[:], in_=cfull[:]),
               waits=[(s_d, P.cnt[s_d]), (s_a, P.cnt[s_a])], inc=s_d)
        P.emit("vector", lambda e: e.max_index(out=i8[:], in_max=t8[:], in_values=cfull[:]),
               waits=[(s_d, P.cnt[s_d]), (s_g, P.cnt[s_g])], inc=s_d)
        state["selread_g"] = P.cnt[s_d]
        P.emit("vector", lambda e: e.tensor_copy(out=t8f[:], in_=t8[:]),
               waits=[(s_d, P.cnt[s_d])], inc=s_d)
        P.emit("vector", lambda e: e.tensor_scalar(
            out=nmx[:], in0=t8f[:, 0:1], scalar1=-1.0, scalar2=None, op0=ALU.mult),
            waits=[(s_d, P.cnt[s_d])], inc=s_d)
        P.emit("scalar", lambda e: e.activation(
            out=e8[:], in_=t8f[:], func=AF.Exp, bias=nmx[:], accum_out=z8[:]),
            waits=[(s_d, P.cnt[s_d]), (s_a, P.cnt[s_a])], inc=s_a)
        P.emit("vector", lambda e: e.reciprocal(out=rz[:], in_=z8[:]),
               waits=[(s_a, P.cnt[s_a]), (s_d, P.cnt[s_d])], inc=s_d)
        P.emit("vector", lambda e: e.tensor_scalar(
            out=w8[:], in0=e8[:], scalar1=rz[:], scalar2=None, op0=ALU.mult),
            waits=[(s_d, P.cnt[s_d])], inc=s_d)
        w8done = P.cnt[s_d]
        P.emit("gpsimd", lambda e, g=g: e.iota(
            ibase[:], pattern=[[0, 1]], base=g * 128 * 8192, channel_multiplier=8192),
            waits=[(s_g, P.cnt[s_g])], inc=s_g)
        for k in range(8):
            kb = k % 2
            P.emit("gpsimd", lambda e, k=k, kb=kb: e.tensor_tensor(
                out=ixk[:, kb, :], in0=ibase[:], in1=i8[:, k:k + 1], op=ALU.add),
                waits=[(s_g, P.cnt[s_g]), (s_d, w8done),
                       (s_rl[kb], state["glast"][kb])], inc=s_g)
            gw = [(s_g, P.cnt[s_g])]
            if k == 0:
                gw += [(s_vv[b], 16 * 2 * (g // 2 + 1))]
            if k >= 2:
                gw.append((s_d, state["rollread"][kb]))
            if _os.environ.get("L0A_NOGATHER"):
                P.emit("gpsimd", lambda e, kb=kb, g=g: e.dma_start(
                    out=roll[:, kb, :], in_=vv_d[g * 128:(g + 1) * 128, 0:D]),
                    waits=gw, inc=s_rl[kb], incn=16)
            else:
                P.emit("gpsimd", lambda e, kb=kb: e.indirect_dma_start(
                    out=roll[:, kb, :], out_offset=None, in_=vv_d[:],
                    in_offset=IndirectOffsetOnAxis(ixk[:, kb, :], axis=1)),
                    waits=gw, inc=s_rl[kb], incn=16)
            state["glast"][kb] = P.cnt[s_rl[kb]]
            rw = [(s_rl[kb], P.cnt[s_rl[kb]]), (s_d, P.cnt[s_d])]
            if k == 0:
                P.emit("vector", lambda e, kb=kb: e.tensor_scalar(
                    out=agg[:], in0=roll[:, kb, :], scalar1=w8[:, 0:1], scalar2=None,
                    op0=ALU.mult), waits=rw, inc=s_d)
            else:
                P.emit("vector", lambda e, k=k, kb=kb: e.tensor_scalar(
                    out=rollw[:], in0=roll[:, kb, :], scalar1=w8[:, k:k + 1], scalar2=None,
                    op0=ALU.mult), waits=rw, inc=s_d)
                P.emit("vector", lambda e: e.tensor_tensor(
                    out=agg[:], in0=agg[:], in1=rollw[:], op=ALU.add),
                    waits=[(s_d, P.cnt[s_d])], inc=s_d)
            state["rollread"][kb] = P.cnt[s_d]
        P.emit("vector", lambda e: e.tensor_scalar(
            out=rollw[:], in0=agg[:], scalar1=0.0, scalar2=None,
            op0=ALU.add, op1=ALU.add, accum_out=st[:, 0:1]),
            waits=[(s_d, P.cnt[s_d])], inc=s_d)
        P.emit("scalar", lambda e: e.activation(
            out=cfull[:], in_=agg[:], func=AF.Square, accum_out=st[:, 1:2]),
            waits=[(s_d, P.cnt[s_d]), (s_a, P.cnt[s_a])], inc=s_a)
        emit_ln_scalars("pln")
        P.emit("scalar", lambda e: e.activation(
            out=x1t[:], in_=agg[:], func=AF.Identity, scale=st[:, 7:8], bias=st[:, 8:9]),
            waits=[(s_d, P.cnt[s_d]), (s_a, P.cnt[s_a]), (s_out, state["t9_acc"])],
            inc=s_a)
        import os as _os2
        if _os2.environ.get("L0A_NOACC"):
            P.emit("gpsimd", lambda e, g=g: e.dma_start(
                out=x1_d[g * 128:(g + 1) * 128, :], in_=x1t[:]),
                waits=[(s_a, P.cnt[s_a]), (s_xc, 16 * 16)], inc=s_out, incn=16)
            P.emit("gpsimd", lambda e: e.engine_nop(), inc=s_out, incn=16)
        else:
            for hh in range(2):
                P.emit("gpsimd", lambda e, g=g, hh=hh: e.dma_start(
                    out=x1_d[g * 128:(g + 1) * 128, hh * 2048:(hh + 1) * 2048],
                    in_=x1t[:, hh * 2048:(hh + 1) * 2048], accum_op=ALU.add),
                    waits=[(s_a, P.cnt[s_a]), (s_xc, 16 * 16)] if hh == 0 else [],
                    inc=s_out, incn=16)
        state["t9_acc"] = P.cnt[s_out]

    for tau in range(4):
        for c in range(4):
            emit_ln(tau * 4 + c)
        emit_dft(tau)
        for c in range(4):
            emit_corr(tau, c)

    P.emit("sync", None, waits=[(s_out, 16 * 32)])
    return P.build()



def build_l0f():
    nc = bass.Bass("TRN2", target_bir_lowering=False)
    x1_d = nc.dram_tensor("x1", [T, D], F32, kind="ExternalInput").ap()
    w1t_d = nc.dram_tensor("w1t", [D, HF], BF16, kind="ExternalInput").ap()
    w2t_d = nc.dram_tensor("w2t", [D, HF], BF16, kind="ExternalInput").ap()
    w3t_d = nc.dram_tensor("w3t", [HF, D], BF16, kind="ExternalInput").ap()
    x2_d = nc.dram_tensor("x2", [T, D], F32, kind="ExternalOutput").ap()
    ha_d = nc.dram_tensor("hact", [86, 128, T], BF16, kind="Internal").ap()

    P = Prog(nc)
    st = P.sb("st", [128, 12], F32)
    junk = P.sb("junk", [128, D], BF16)
    junka = P.sb("junka", [128, D], BF16)
    gpsum = P.ps("gpsum", [128, 2, 512], F32)
    upsum = P.ps("upsum", [128, 2, 512], F32)
    acps = P.ps("acps", [128, 2, 512], F32)

    s_xl = [P.sem("s_xl0"), P.sem("s_xl1")]
    s_d = P.sem("s_d")
    s_a = P.sem("s_a")
    s_tp = [P.sem("s_tp0"), P.sem("s_tp1")]
    s_w12 = [P.sem("s_w120"), P.sem("s_w121")]
    s_mm = P.sem("s_mm")
    s_hst = [P.sem("s_hst0"), P.sem("s_hst1")]
    s_ha = P.sem("s_ha")
    s_w3 = [P.sem("s_w30"), P.sem("s_w31")]
    s_ytp = [P.sem("s_ytp0"), P.sem("s_ytp1")]
    s_xc = P.sem("s_xc")
    s_out = P.sem("s_out")

    LNW = 1.0 / 4096.0
    EPS = 1e-5
    state = {"lnapply0": 0, "lnapply1": 0}

    for g in range(16):
        P.emit("gpsimd", lambda e, g=g: e.dma_start(
            out=x2_d[g * 128:(g + 1) * 128, :], in_=x1_d[g * 128:(g + 1) * 128, :]),
            inc=s_xc, incn=16)

    def emit_ln_scalars():
        P.emit("vector", lambda e: e.tensor_scalar(
            out=st[:, 2:3], in0=st[:, 0:1], scalar1=LNW, scalar2=None, op0=ALU.mult),
            waits=[(s_d, P.cnt[s_d])], inc=s_d)
        P.emit("vector", lambda e: e.tensor_scalar(
            out=st[:, 3:4], in0=st[:, 1:2], scalar1=LNW, scalar2=EPS,
            op0=ALU.mult, op1=ALU.add),
            waits=[(s_a, P.cnt[s_a]), (s_d, P.cnt[s_d])], inc=s_d)
        P.emit("vector", lambda e: e.tensor_tensor(
            out=st[:, 4:5], in0=st[:, 2:3], in1=st[:, 2:3], op=ALU.mult),
            waits=[(s_d, P.cnt[s_d])], inc=s_d)
        P.emit("vector", lambda e: e.tensor_tensor(
            out=st[:, 5:6], in0=st[:, 3:4], in1=st[:, 4:5], op=ALU.subtract),
            waits=[(s_d, P.cnt[s_d])], inc=s_d)
        P.emit("scalar", lambda e: e.activation(
            out=st[:, 6:7], in_=st[:, 5:6], func=AF.Ln),
            waits=[(s_d, P.cnt[s_d]), (s_a, P.cnt[s_a])], inc=s_a)
        nhalf = P.constf(-0.5)
        P.emit("scalar", lambda e, nhalf=nhalf: e.activation(
            out=st[:, 7:8], in_=st[:, 6:7], func=AF.Exp, scale=nhalf),
            waits=[(s_a, P.cnt[s_a]), P.cw()], inc=s_a)
        P.emit("vector", lambda e: e.scalar_tensor_tensor(
            out=st[:, 8:9], in0=st[:, 2:3], scalar=-1.0, in1=st[:, 7:8],
            op0=ALU.mult, op1=ALU.mult),
            waits=[(s_a, P.cnt[s_a]), (s_d, P.cnt[s_d])], inc=s_d)

    def emit_ln(g, xsrc, xfm, c, mmgate):
        b = g % 2
        nl = state.setdefault("nxl", [0, 0])
        nl[b] += 1
        P.emit("gpsimd", lambda e, g=g, b=b, xsrc=xsrc: e.dma_start(
            out=xb[:, b, :], in_=xsrc[g * 128:(g + 1) * 128, :]),
            waits=[(s_a, state["lnapply%d" % b])], inc=s_xl[b], incn=16)
        P.emit("vector", lambda e, b=b: e.tensor_scalar(
            out=junk[:], in0=xb[:, b, :], scalar1=0.0, scalar2=None,
            op0=ALU.add, op1=ALU.add, accum_out=st[:, 0:1]),
            waits=[(s_xl[b], 16 * nl[b]), (s_d, P.cnt[s_d])], inc=s_d)
        P.emit("scalar", lambda e, b=b: e.activation(
            out=junka[:], in_=xb[:, b, :], func=AF.Square, accum_out=st[:, 1:2]),
            waits=[(s_xl[b], 16 * nl[b]), (s_a, P.cnt[s_a])], inc=s_a)
        emit_ln_scalars()
        vw = [(s_d, P.cnt[s_d]), (s_a, P.cnt[s_a]),
              (s_tp[b], state.setdefault("vlntp%d" % b, 0))]
        P.emit("scalar", lambda e, b=b: e.activation(
            out=vln[:, b, :], in_=xb[:, b, :], func=AF.Identity,
            scale=st[:, 7:8], bias=st[:, 8:9]),
            waits=vw, inc=s_a)
        state["lnapply%d" % b] = P.cnt[s_a]
        aval = P.cnt[s_a]
        for k in range(32):
            tw = [(s_a, aval)] if k == 0 else []
            if k == 0 and mmgate:
                tw.append((s_mm, mmgate))
            P.emit("scalar", lambda e, b=b, c=c, k=k, xfm=xfm: e.dma_start(
                out=xfm[:, k, c * 128:(c + 1) * 128],
                in_=vln[:, b, k * 128:(k + 1) * 128], transpose=True),
                waits=tw, inc=s_tp[b], incn=16)
        state["vlntp%d" % b] = P.cnt[s_tp[b]]

    xb = P.sb("xb", [128, 2, D], BF16)
    vln = P.sb("vln", [128, 2, D], BF16)
    xfm = P.sb("xfm", [128, 32, 1024], BF16)
    w1ts = P.sb("w1ts", [128, 2, 32, 128], BF16)
    w2ts = P.sb("w2ts", [128, 2, 32, 128], BF16)
    sg = P.sb("sg", [128, 2, 1024], BF16)
    htile = P.sb("htile", [128, 2, 1024], BF16)

    for th in range(2):
        mmgate = P.cnt[s_mm]
        for c in range(8):
            emit_ln(th * 8 + c, x1_d, xfm, c, mmgate if c == 0 else 0)
        tpw = [(s_tp[0], P.cnt[s_tp[0]]), (s_tp[1], P.cnt[s_tp[1]])]
        nw12 = state.setdefault("nw12", [0, 0])
        for ht in range(86):
            pb = ht % 2
            nw12[pb] += 1
            lw = [(s_mm, state.setdefault("w12mm%d" % pb, 0))]
            P.emit("sync", lambda e, ht=ht, pb=pb: e.dma_start(
                out=w1ts[:, pb, :, :],
                in_=w1t_d.rearrange("(c p) h -> p c h", p=128)[:, :, ht * 128:(ht + 1) * 128]),
                waits=lw, inc=s_w12[pb], incn=16)
            P.emit("sync", lambda e, ht=ht, pb=pb: e.dma_start(
                out=w2ts[:, pb, :, :],
                in_=w2t_d.rearrange("(c p) h -> p c h", p=128)[:, :, ht * 128:(ht + 1) * 128]),
                inc=s_w12[pb], incn=16)
            mmw = [(s_w12[pb], P.cnt[s_w12[pb]]),
                   (s_a, state.setdefault("silu_a", 0)),
                   (s_d, state.setdefault("hmul_d", 0))] + tpw
            for q in range(2):
                for dc in range(32):
                    P.emit("tensor", lambda e, pb=pb, dc=dc, q=q: e.matmul(
                        out=gpsum[:, q, :], lhsT=w1ts[:, pb, dc, :],
                        rhs=xfm[:, dc, q * 512:(q + 1) * 512],
                        start=(dc == 0), stop=(dc == 31)),
                        waits=mmw if (dc == 0 and q == 0) else [],
                        inc=s_mm if dc == 31 else None)
                for dc in range(32):
                    P.emit("tensor", lambda e, pb=pb, dc=dc, q=q: e.matmul(
                        out=upsum[:, q, :], lhsT=w2ts[:, pb, dc, :],
                        rhs=xfm[:, dc, q * 512:(q + 1) * 512],
                        start=(dc == 0), stop=(dc == 31)),
                        inc=s_mm if dc == 31 else None)
            state["w12mm%d" % pb] = P.cnt[s_mm]
            mmv = P.cnt[s_mm]
            b2 = ht % 2
            for q in range(2):
                P.emit("scalar", lambda e, b2=b2, q=q: e.activation(
                    out=sg[:, b2, q * 512:(q + 1) * 512], in_=gpsum[:, q, :], func=AF.Silu),
                    waits=[(s_mm, mmv - (3 - 2 * q)), (s_a, P.cnt[s_a])] +
                          ([(s_d, state.setdefault("hmul%d" % b2, 0))] if q == 0 else []),
                    inc=s_a)
                state["silu_a"] = P.cnt[s_a]
                P.emit("vector", lambda e, b2=b2, q=q: e.tensor_tensor(
                    out=htile[:, b2, q * 512:(q + 1) * 512],
                    in0=sg[:, b2, q * 512:(q + 1) * 512], in1=upsum[:, q, :], op=ALU.mult),
                    waits=[(s_mm, mmv - (2 - 2 * q)), (s_a, P.cnt[s_a]), (s_d, P.cnt[s_d])] +
                          ([(s_hst[b2], state.setdefault("hst%d" % b2, 0))] if q == 0 else []),
                    inc=s_d)
                state["hmul_d"] = P.cnt[s_d]
            state["hmul%d" % b2] = P.cnt[s_d]
            P.emit("gpsimd", lambda e, ht=ht, th=th, b2=b2: e.dma_start(
                out=ha_d[ht, :, th * 1024:(th + 1) * 1024], in_=htile[:, b2, :]),
                waits=[(s_d, P.cnt[s_d])], inc=s_hst[b2], incn=16)
            state["hst%d" % b2] = P.cnt[s_hst[b2]]

    P.sbfree("htile", "sg", "w2ts", "w1ts", "xfm", "vln", "xb")

    hacts = P.sb("hacts", [128, 86, 512], BF16)
    w3ts = P.sb("w3ts", [128, 2, 86, 128], BF16)
    yfm = P.sb("yfm", [128, 32, 512], BF16)
    ytm = P.sb("ytm", [128, 1, D], BF16)
    x2t = P.sb("x2t", [128, D], F32)

    hstw = [(s_hst[0], P.cnt[s_hst[0]]), (s_hst[1], P.cnt[s_hst[1]])]
    for ts in range(4):
        P.emit("sync", lambda e, ts=ts: e.dma_start(
            out=hacts[:], in_=ha_d.rearrange("h p t -> p h t")[:, :, ts * 512:(ts + 1) * 512]),
            waits=hstw + [(s_mm, P.cnt[s_mm])], inc=s_ha, incn=16)
        for dt in range(32):
            pb = dt % 2
            state.setdefault("nw3", [0, 0])[pb] += 1
            P.emit("sync", lambda e, dt=dt, pb=pb: e.dma_start(
                out=w3ts[:, pb, :, :],
                in_=w3t_d.rearrange("(c p) d -> p c d", p=128)[:, :, dt * 128:(dt + 1) * 128]),
                waits=[(s_mm, state.setdefault("w3mm%d" % pb, 0))], inc=s_w3[pb], incn=16)
            mmw = [(s_w3[pb], 16 * state["nw3"][pb]), (s_ha, 16 * (ts + 1)),
                   (s_a, state.setdefault("yev%d" % pb, 0))]
            for hc in range(86):
                P.emit("tensor", lambda e, pb=pb, hc=hc: e.matmul(
                    out=acps[:, pb, :], lhsT=w3ts[:, pb, hc, :], rhs=hacts[:, hc, :],
                    start=(hc == 0), stop=(hc == 85)),
                    waits=mmw if hc == 0 else [], inc=s_mm if hc == 85 else None)
            state["w3mm%d" % pb] = P.cnt[s_mm]
            P.emit("scalar", lambda e, dt=dt, pb=pb: e.activation(
                out=yfm[:, dt, :], in_=acps[:, pb, :], func=AF.Copy),
                waits=[(s_mm, P.cnt[s_mm]), (s_a, P.cnt[s_a]),
                       (s_ytp[0], state.setdefault("ytpA", 0)),
                       (s_ytp[1], state.setdefault("ytpB", 0))], inc=s_a)
            state["yev%d" % pb] = P.cnt[s_a]
        aval = P.cnt[s_a]
        for c in range(4):
            tc = ts * 4 + c
            cb = 0
            for k in range(32):
                tw = [(s_a, max(aval, state.setdefault("x2apply", 0)))] if k == 0 else []
                if k == 0:
                    tw.append((s_d, state.setdefault("pln%d" % cb, 0)))
                P.emit("scalar", lambda e, cb=cb, c=c, k=k: e.dma_start(
                    out=ytm[:, cb, k * 128:(k + 1) * 128],
                    in_=yfm[:, k, c * 128:(c + 1) * 128], transpose=True),
                    waits=tw, inc=s_ytp[cb], incn=16)
            if cb == 0:
                state["ytpA"] = P.cnt[s_ytp[0]]
            else:
                state["ytpB"] = P.cnt[s_ytp[1]]
            P.emit("vector", lambda e, cb=cb: e.tensor_scalar(
                out=junk[:], in0=ytm[:, cb, :], scalar1=0.0, scalar2=None,
                op0=ALU.add, op1=ALU.add, accum_out=st[:, 0:1]),
                waits=[(s_ytp[cb], P.cnt[s_ytp[cb]]), (s_d, P.cnt[s_d])], inc=s_d)
            P.emit("scalar", lambda e, cb=cb: e.activation(
                out=junka[:], in_=ytm[:, cb, :], func=AF.Square, accum_out=st[:, 1:2]),
                waits=[(s_ytp[cb], P.cnt[s_ytp[cb]]), (s_a, P.cnt[s_a])], inc=s_a)
            emit_ln_scalars()
            P.emit("scalar", lambda e, cb=cb: e.activation(
                out=x2t[:], in_=ytm[:, cb, :], func=AF.Identity,
                scale=st[:, 7:8], bias=st[:, 8:9]),
                waits=[(s_d, P.cnt[s_d]), (s_a, P.cnt[s_a]),
                       (s_out, state.setdefault("acc_out", 0))], inc=s_a)
            state["x2apply"] = P.cnt[s_a]
            for hh in range(2):
                P.emit("gpsimd", lambda e, tc=tc, hh=hh: e.dma_start(
                    out=x2_d[tc * 128:(tc + 1) * 128, hh * 2048:(hh + 1) * 2048],
                    in_=x2t[:, hh * 2048:(hh + 1) * 2048], accum_op=ALU.add),
                    waits=[(s_a, P.cnt[s_a]), (s_xc, 16 * 16)] if hh == 0 else [],
                    inc=s_out, incn=16)
            state["acc_out"] = P.cnt[s_out]
            state["pln%d" % cb] = P.cnt[s_d]

    P.emit("sync", None, waits=[(s_out, P.cnt[s_out])])
    return P.build()



def build_l1f():
    nc = bass.Bass("TRN2", target_bir_lowering=False)
    x3_d = nc.dram_tensor("x3", [T, D], F32, kind="ExternalInput").ap()
    at_d = nc.dram_tensor("att", [D, 128], BF16, kind="ExternalInput").ap()
    bst_d = nc.dram_tensor("bst", [128, D], BF16, kind="ExternalInput").ap()
    wrt_d = nc.dram_tensor("wrt", [D, 128], BF16, kind="ExternalInput").ap()
    brb_d = nc.dram_tensor("brb", [128, 8], F32, kind="ExternalInput").ap()
    em_d = nc.dram_tensor("emat", [128, 128], BF16, kind="ExternalInput").ap()
    xo_d = nc.dram_tensor("xo", [T, D], F32, kind="ExternalOutput").ap()

    P = Prog(nc)
    st = P.sb("st", [128, 12], F32)
    junk = P.sb("junk", [128, D], BF16)
    junka = P.sb("junka", [128, D], BF16)
    xb = P.sb("xb", [128, 2, D], BF16)
    vln = P.sb("vln", [128, 2, D], BF16)
    xfm = P.sb("xfm", [128, 32, 512], BF16)
    atts = P.sb("atts", [128, 32, 128], BF16)
    bsts = P.sb("bsts", [128, 32, 128], BF16)
    wrts = P.sb("wrts", [128, 32, 128], BF16)
    brb = P.sb("brbs", [128, 8], F32)
    emat = P.sb("emats", [128, 128], BF16)
    lgb = P.sb("lgb", [128, 8], F32)
    e8 = P.sb("e8", [128, 8], F32)
    sc32 = P.sb("sc32", [128, 4, 128], BF16)
    scT = P.sb("scT", [128, 512], BF16)
    sS = P.sb("sS", [128, 512], BF16)
    lowp = P.sb("lowp", [128, 512], BF16)
    yfm = P.sb("yfm", [128, 32, 512], BF16)
    ytm = P.sb("ytm", [128, D], BF16)
    x2t = P.sb("x2t", [128, D], F32)
    lgA = P.ps("lgA", [128, 128], F32)
    lgB = P.ps("lgB", [128, 128], F32)
    lw = P.ps("lw", [128, 512], F32)
    sp = P.ps("sp", [128, 512], F32)
    ypA = P.ps("ypA", [128, 512], F32)
    ypB = P.ps("ypB", [128, 512], F32)

    s_xl = [P.sem("s_xl0"), P.sem("s_xl1")]
    s_d = P.sem("s_d")
    s_a = P.sem("s_a")
    s_tp = [P.sem("s_tp0"), P.sem("s_tp1")]
    s_wl = P.sem("s_wl")
    s_mm = P.sem("s_mm")
    s_sct = P.sem("s_sct")
    s_ytp = P.sem("s_ytp")
    s_xc = P.sem("s_xc")
    s_out = P.sem("s_out")

    LNW = 1.0 / 4096.0
    EPS = 1e-5
    state = {"lnapply0": 0, "lnapply1": 0}

    for g in range(16):
        P.emit("gpsimd", lambda e, g=g: e.dma_start(
            out=xo_d[g * 128:(g + 1) * 128, :], in_=x3_d[g * 128:(g + 1) * 128, :]),
            inc=s_xc, incn=16)
    P.emit("sync", lambda e: e.dma_start(
        out=atts[:], in_=at_d.rearrange("(c p) h -> p c h", p=128)), inc=s_wl, incn=16)
    P.emit("sync", lambda e: e.dma_start(
        out=bsts[:], in_=bst_d.rearrange("h (c p) -> h c p", p=128).rearrange("h c p -> p c h") if False else bst_d.rearrange("p (c d) -> p c d", d=128)), inc=s_wl, incn=16)
    P.emit("sync", lambda e: e.dma_start(
        out=wrts[:], in_=wrt_d.rearrange("(c p) h -> p c h", p=128)), inc=s_wl, incn=16)
    P.emit("sync", lambda e: e.dma_start(out=brb[:], in_=brb_d[:]), inc=s_wl, incn=16)
    P.emit("sync", lambda e: e.dma_start(out=emat[:], in_=em_d[:]), inc=s_wl, incn=16)
    P.emit("gpsimd", lambda e: e.memset(sc32[:], 0.0), inc=P.sem("s_ms"), incn=1)
    s_ms = "s_ms"

    def emit_ln_scalars():
        P.emit("vector", lambda e: e.tensor_scalar(
            out=st[:, 2:3], in0=st[:, 0:1], scalar1=LNW, scalar2=None, op0=ALU.mult),
            waits=[(s_d, P.cnt[s_d])], inc=s_d)
        P.emit("vector", lambda e: e.tensor_scalar(
            out=st[:, 3:4], in0=st[:, 1:2], scalar1=LNW, scalar2=EPS,
            op0=ALU.mult, op1=ALU.add),
            waits=[(s_a, P.cnt[s_a]), (s_d, P.cnt[s_d])], inc=s_d)
        P.emit("vector", lambda e: e.tensor_tensor(
            out=st[:, 4:5], in0=st[:, 2:3], in1=st[:, 2:3], op=ALU.mult),
            waits=[(s_d, P.cnt[s_d])], inc=s_d)
        P.emit("vector", lambda e: e.tensor_tensor(
            out=st[:, 5:6], in0=st[:, 3:4], in1=st[:, 4:5], op=ALU.subtract),
            waits=[(s_d, P.cnt[s_d])], inc=s_d)
        P.emit("scalar", lambda e: e.activation(
            out=st[:, 6:7], in_=st[:, 5:6], func=AF.Ln),
            waits=[(s_d, P.cnt[s_d]), (s_a, P.cnt[s_a])], inc=s_a)
        nhalf = P.constf(-0.5)
        P.emit("scalar", lambda e, nhalf=nhalf: e.activation(
            out=st[:, 7:8], in_=st[:, 6:7], func=AF.Exp, scale=nhalf),
            waits=[(s_a, P.cnt[s_a]), P.cw()], inc=s_a)
        P.emit("vector", lambda e: e.scalar_tensor_tensor(
            out=st[:, 8:9], in0=st[:, 2:3], scalar=-1.0, in1=st[:, 7:8],
            op0=ALU.mult, op1=ALU.mult),
            waits=[(s_a, P.cnt[s_a]), (s_d, P.cnt[s_d])], inc=s_d)

    nl = [0, 0]
    for ts in range(4):
        mmgate = state.get("xfm_mm", 0)
        for c in range(4):
            g = ts * 4 + c
            b = g % 2
            nl[b] += 1
            P.emit("gpsimd", lambda e, g=g, b=b: e.dma_start(
                out=xb[:, b, :], in_=x3_d[g * 128:(g + 1) * 128, :]),
                waits=[(s_a, state["lnapply%d" % b])], inc=s_xl[b], incn=16)
            P.emit("vector", lambda e, b=b: e.tensor_scalar(
                out=junk[:], in0=xb[:, b, :], scalar1=0.0, scalar2=None,
                op0=ALU.add, op1=ALU.add, accum_out=st[:, 0:1]),
                waits=[(s_xl[b], 16 * nl[b]), (s_d, P.cnt[s_d])], inc=s_d)
            P.emit("scalar", lambda e, b=b: e.activation(
                out=junka[:], in_=xb[:, b, :], func=AF.Square, accum_out=st[:, 1:2]),
                waits=[(s_xl[b], 16 * nl[b]), (s_a, P.cnt[s_a])], inc=s_a)
            emit_ln_scalars()
            vw = [(s_d, P.cnt[s_d]), (s_a, P.cnt[s_a]),
                  (s_tp[b], state.setdefault("vlntp%d" % b, 0))]
            P.emit("scalar", lambda e, b=b: e.activation(
                out=vln[:, b, :], in_=xb[:, b, :], func=AF.Identity,
                scale=st[:, 7:8], bias=st[:, 8:9]),
                waits=vw, inc=s_a)
            state["lnapply%d" % b] = P.cnt[s_a]
            aval = P.cnt[s_a]
            for k in range(32):
                tw = [(s_a, aval)] if k == 0 else []
                if k == 0 and c == 0 and mmgate:
                    tw.append((s_mm, mmgate))
                P.emit("scalar", lambda e, b=b, c=c, k=k: e.dma_start(
                    out=xfm[:, k, c * 128:(c + 1) * 128],
                    in_=vln[:, b, k * 128:(k + 1) * 128], transpose=True),
                    waits=tw, inc=s_tp[b], incn=16)
            state["vlntp%d" % b] = P.cnt[s_tp[b]]
        tpw = [(s_tp[0], P.cnt[s_tp[0]]), (s_tp[1], P.cnt[s_tp[1]])]
        for c in range(4):
            pb = c % 2
            mmw = tpw + [(s_wl, 16 * 5), (s_d, state.setdefault("lgread%d" % pb, 0))]
            for dc in range(32):
                P.emit("tensor", lambda e, c=c, pb=pb, dc=dc: e.matmul(
                    out=(lgA if pb == 0 else lgB)[:], lhsT=xfm[:, dc, c * 128:(c + 1) * 128],
                    rhs=wrts[:, dc, :], start=(dc == 0), stop=(dc == 31)),
                    waits=mmw if dc == 0 else [], inc=s_mm if dc == 31 else None)
            mmv = P.cnt[s_mm]
            P.emit("vector", lambda e, pb=pb: e.tensor_tensor(
                out=lgb[:], in0=(lgA if pb == 0 else lgB)[:, 0:8], in1=brb[:], op=ALU.add),
                waits=[(s_mm, mmv), (s_d, P.cnt[s_d])], inc=s_d)
            state["lgread%d" % pb] = P.cnt[s_d]
            P.emit("vector", lambda e: e.tensor_scalar(
                out=junk[:, 0:8], in0=lgb[:], scalar1=0.0, scalar2=None,
                op0=ALU.add, op1=ALU.max, accum_out=st[:, 9:10]),
                waits=[(s_d, P.cnt[s_d])], inc=s_d)
            P.emit("vector", lambda e: e.tensor_scalar(
                out=st[:, 10:11], in0=st[:, 9:10], scalar1=-1.0, scalar2=None,
                op0=ALU.mult), waits=[(s_d, P.cnt[s_d])], inc=s_d)
            P.emit("scalar", lambda e: e.activation(
                out=e8[:], in_=lgb[:], func=AF.Exp, bias=st[:, 10:11],
                accum_out=st[:, 11:12]),
                waits=[(s_d, P.cnt[s_d]), (s_a, P.cnt[s_a])], inc=s_a)
            P.emit("vector", lambda e: e.reciprocal(out=st[:, 9:10], in_=st[:, 11:12]),
                   waits=[(s_a, P.cnt[s_a]), (s_d, P.cnt[s_d])], inc=s_d)
            P.emit("vector", lambda e, c=c: e.tensor_scalar(
                out=sc32[:, c, 0:8], in0=e8[:], scalar1=st[:, 9:10], scalar2=16.0,
                op0=ALU.mult, op1=ALU.mult),
                waits=[(s_d, P.cnt[s_d]), (s_sct, state.setdefault("sctd", 0))], inc=s_d)
        scdone = P.cnt[s_d]
        for c in range(4):
            P.emit("scalar", lambda e, c=c: e.dma_start(
                out=scT[:, c * 128:(c + 1) * 128], in_=sc32[:, c, :], transpose=True),
                waits=[(s_d, scdone), (s_ms, 1)] if c == 0 else [], inc=s_sct, incn=16)
        state["sctd"] = P.cnt[s_sct]
        P.emit("tensor", lambda e: e.matmul(
            out=sp[:], lhsT=emat[:], rhs=scT[:], start=True, stop=True),
            waits=[(s_sct, P.cnt[s_sct]), (s_a, state.setdefault("spev", 0)),
                   (s_wl, 16 * 5)], inc=s_mm)
        P.emit("scalar", lambda e: e.activation(
            out=sS[:], in_=sp[:], func=AF.Copy),
            waits=[(s_mm, P.cnt[s_mm]), (s_a, P.cnt[s_a])], inc=s_a)
        state["spev"] = P.cnt[s_a]
        mmw2 = [(s_d, state.setdefault("lowread", 0))] + tpw
        for dc in range(32):
            P.emit("tensor", lambda e, dc=dc: e.matmul(
                out=lw[:], lhsT=atts[:, dc, :], rhs=xfm[:, dc, :],
                start=(dc == 0), stop=(dc == 31)),
                waits=mmw2 if dc == 0 else [], inc=s_mm if dc == 31 else None)
        P.emit("vector", lambda e: e.tensor_tensor(
            out=lowp[:], in0=lw[:], in1=sS[:], op=ALU.mult),
            waits=[(s_mm, P.cnt[s_mm]), (s_a, P.cnt[s_a]), (s_d, P.cnt[s_d]),
                   (s_mm, state.setdefault("yread", 0))], inc=s_d)
        state["lowread"] = P.cnt[s_d]
        for dt in range(32):
            pb = dt % 2
            P.emit("tensor", lambda e, dt=dt, pb=pb: e.matmul(
                out=(ypA if pb == 0 else ypB)[:], lhsT=bsts[:, dt, :], rhs=lowp[:],
                start=True, stop=True),
                waits=[(s_d, P.cnt[s_d]), (s_a, state.setdefault("ypev%d" % pb, 0))],
                inc=s_mm)
            P.emit("scalar", lambda e, dt=dt, pb=pb: e.activation(
                out=yfm[:, dt, :], in_=(ypA if pb == 0 else ypB)[:], func=AF.Copy),
                waits=[(s_mm, P.cnt[s_mm]), (s_a, P.cnt[s_a]),
                       (s_ytp, state.setdefault("ytpd", 0))], inc=s_a)
            state["ypev%d" % pb] = P.cnt[s_a]
        state["yread"] = P.cnt[s_mm]
        aval2 = P.cnt[s_a]
        for c in range(4):
            tc = ts * 4 + c
            for k in range(32):
                tw = [(s_a, max(aval2, state.setdefault("x2apply", 0))),
                      (s_d, state.setdefault("плн", 0))] if k == 0 else []
                P.emit("scalar", lambda e, c=c, k=k: e.dma_start(
                    out=ytm[:, k * 128:(k + 1) * 128],
                    in_=yfm[:, k, c * 128:(c + 1) * 128], transpose=True),
                    waits=tw, inc=s_ytp, incn=16)
            state["ytpd"] = P.cnt[s_ytp]
            P.emit("vector", lambda e: e.tensor_scalar(
                out=junk[:], in0=ytm[:], scalar1=0.0, scalar2=None,
                op0=ALU.add, op1=ALU.add, accum_out=st[:, 0:1]),
                waits=[(s_ytp, P.cnt[s_ytp]), (s_d, P.cnt[s_d])], inc=s_d)
            P.emit("scalar", lambda e: e.activation(
                out=junka[:], in_=ytm[:], func=AF.Square, accum_out=st[:, 1:2]),
                waits=[(s_ytp, P.cnt[s_ytp]), (s_a, P.cnt[s_a])], inc=s_a)
            emit_ln_scalars()
            P.emit("scalar", lambda e: e.activation(
                out=x2t[:], in_=ytm[:], func=AF.Identity,
                scale=st[:, 7:8], bias=st[:, 8:9]),
                waits=[(s_d, P.cnt[s_d]), (s_a, P.cnt[s_a]),
                       (s_out, state.setdefault("acc_out", 0))], inc=s_a)
            state["x2apply"] = P.cnt[s_a]
            state["плн"] = P.cnt[s_d]
            for hh in range(2):
                P.emit("gpsimd", lambda e, tc=tc, hh=hh: e.dma_start(
                    out=xo_d[tc * 128:(tc + 1) * 128, hh * 2048:(hh + 1) * 2048],
                    in_=x2t[:, hh * 2048:(hh + 1) * 2048], accum_op=ALU.add),
                    waits=[(s_a, P.cnt[s_a]), (s_xc, 16 * 16)] if hh == 0 else [],
                    inc=s_out, incn=16)
            state["acc_out"] = P.cnt[s_out]
        state["xfm_mm"] = P.cnt[s_mm]

    P.emit("sync", None, waits=[(s_out, P.cnt[s_out])])
    return P.build()



def _ceil(a, b):
    return -(-a // b)


NUM_REMAIN = 205
SCL = float(np.float32(1.0) / np.float32(np.sqrt(np.float32(128.0))))


def build_l1a():
    nc = bass.Bass("TRN2", target_bir_lowering=False)
    x2_d = nc.dram_tensor("x2", [T, D], F32, kind="ExternalInput").ap()
    wq_d = nc.dram_tensor("wqT", [D, D], BF16, kind="ExternalInput").ap()
    wk_d = nc.dram_tensor("wkT", [D, 1024], BF16, kind="ExternalInput").ap()
    wv_d = nc.dram_tensor("wvT", [D, 1024], BF16, kind="ExternalInput").ap()
    wo_d = nc.dram_tensor("woT", [D, D], BF16, kind="ExternalInput").ap()
    cos_d = nc.dram_tensor("cosF", [128, T], BF16, kind="ExternalInput").ap()
    sin_d = nc.dram_tensor("sinF", [128, T], BF16, kind="ExternalInput").ap()
    tri_d = nc.dram_tensor("tri", [128, 128], BF16, kind="ExternalInput").ap()
    ka_d = nc.dram_tensor("keepall", [128, 16], F32, kind="ExternalInput").ap()
    x3_d = nc.dram_tensor("x3", [T, D], F32, kind="ExternalOutput").ap()
    qr_d = nc.dram_tensor("qrs", [32, 128, T], BF16, kind="Internal").ap()

    P = Prog(nc)
    st = P.sb("st", [128, 12], F32)
    junk = P.sb("junk", [128, D], BF16)
    krF = P.sb("krF", [128, 8, T], BF16)
    vTs = P.sb("vTs", [128, 16, 8, 128], BF16)
    cosb = P.sb("cosb", [128, T], BF16)
    sinb = P.sb("sinb", [128, T], BF16)
    trib = P.sb("trib", [128, 128], BF16)
    kab = P.sb("kab", [128, 16], F32)
    ppA = P.ps("ppA", [128, 512], F32)
    ppB = P.ps("ppB", [128, 512], F32)
    dpA = P.ps("dpA", [128, 512], F32)
    dpB = P.ps("dpB", [128, 512], F32)
    avA = P.ps("avA", [128, 128], F32)
    avB = P.ps("avB", [128, 128], F32)
    opA = P.ps("opA", [128, 256], F32)
    opB = P.ps("opB", [128, 256], F32)

    s_xl = [P.sem("s_xl0"), P.sem("s_xl1")]
    s_d = P.sem("s_d")
    s_a = P.sem("s_a")
    s_g = P.sem("s_g")
    s_tp = [P.sem("s_tp0"), P.sem("s_tp1")]
    s_wl = [P.sem("s_wl0"), P.sem("s_wl1")]
    s_cs = P.sem("s_cs")
    s_mm = P.sem("s_mm")
    s_qst = [P.sem("s_qst0"), P.sem("s_qst1")]
    s_ql = [P.sem("s_ql0"), P.sem("s_ql1")]
    s_et = P.sem("s_et")
    s_ytp = P.sem("s_ytp")
    s_xc = P.sem("s_xc")
    s_out = P.sem("s_out")

    LNW = 1.0 / 4096.0
    EPS = 1e-5
    state = {"lnapply0": 0, "lnapply1": 0}

    for g in range(16):
        P.emit("gpsimd", lambda e, g=g: e.dma_start(
            out=x3_d[g * 128:(g + 1) * 128, :], in_=x2_d[g * 128:(g + 1) * 128, :]),
            inc=s_xc, incn=16)
    P.emit("sync", lambda e: e.dma_start(out=cosb[:], in_=cos_d[:]), inc=s_cs, incn=16)
    P.emit("sync", lambda e: e.dma_start(out=sinb[:], in_=sin_d[:]), inc=s_cs, incn=16)
    P.emit("sync", lambda e: e.dma_start(out=trib[:], in_=tri_d[:]), inc=s_cs, incn=16)
    P.emit("sync", lambda e: e.dma_start(out=kab[:], in_=ka_d[:]), inc=s_cs, incn=16)

    def emit_ln_scalars():
        P.emit("vector", lambda e: e.tensor_scalar(
            out=st[:, 2:3], in0=st[:, 0:1], scalar1=LNW, scalar2=None, op0=ALU.mult),
            waits=[(s_d, P.cnt[s_d])], inc=s_d)
        P.emit("vector", lambda e: e.tensor_scalar(
            out=st[:, 3:4], in0=st[:, 1:2], scalar1=LNW, scalar2=EPS,
            op0=ALU.mult, op1=ALU.add),
            waits=[(s_a, P.cnt[s_a]), (s_d, P.cnt[s_d])], inc=s_d)
        P.emit("vector", lambda e: e.tensor_tensor(
            out=st[:, 4:5], in0=st[:, 2:3], in1=st[:, 2:3], op=ALU.mult),
            waits=[(s_d, P.cnt[s_d])], inc=s_d)
        P.emit("vector", lambda e: e.tensor_tensor(
            out=st[:, 5:6], in0=st[:, 3:4], in1=st[:, 4:5], op=ALU.subtract),
            waits=[(s_d, P.cnt[s_d])], inc=s_d)
        P.emit("scalar", lambda e: e.activation(
            out=st[:, 6:7], in_=st[:, 5:6], func=AF.Ln),
            waits=[(s_d, P.cnt[s_d]), (s_a, P.cnt[s_a])], inc=s_a)
        nhalf = P.constf(-0.5)
        P.emit("scalar", lambda e, nhalf=nhalf: e.activation(
            out=st[:, 7:8], in_=st[:, 6:7], func=AF.Exp, scale=nhalf),
            waits=[(s_a, P.cnt[s_a]), P.cw()], inc=s_a)
        P.emit("vector", lambda e: e.scalar_tensor_tensor(
            out=st[:, 8:9], in0=st[:, 2:3], scalar=-1.0, in1=st[:, 7:8],
            op0=ALU.mult, op1=ALU.mult),
            waits=[(s_a, P.cnt[s_a]), (s_d, P.cnt[s_d])], inc=s_d)

    xb = P.sb("xb", [128, 2, D], BF16)
    vln = P.sb("vln", [128, 2, D], BF16)
    junka = P.sb("junka", [128, D], BF16)
    xfm = P.sb("xfm", [128, 32, 512], BF16)
    wts = P.sb("wts", [128, 2, 32, 128], BF16)
    ra = P.sb("ra", [128, 512], F32)
    rb = P.sb("rb", [128, 512], F32)
    qrt = P.sb("qrt", [128, 2, 512], BF16)
    vtmp = P.sb("vtmp", [128, 512], BF16)

    nl = [0, 0]
    nwl = [0, 0]
    for ts in range(4):
        mmgate = state.get("xfm_mm", 0)
        for c in range(4):
            g = ts * 4 + c
            b = g % 2
            nl[b] += 1
            P.emit("gpsimd", lambda e, g=g, b=b: e.dma_start(
                out=xb[:, b, :], in_=x2_d[g * 128:(g + 1) * 128, :]),
                waits=[(s_a, state["lnapply%d" % b])], inc=s_xl[b], incn=16)
            P.emit("vector", lambda e, b=b: e.tensor_scalar(
                out=junk[:], in0=xb[:, b, :], scalar1=0.0, scalar2=None,
                op0=ALU.add, op1=ALU.add, accum_out=st[:, 0:1]),
                waits=[(s_xl[b], 16 * nl[b]), (s_d, P.cnt[s_d])], inc=s_d)
            P.emit("scalar", lambda e, b=b: e.activation(
                out=junka[:], in_=xb[:, b, :], func=AF.Square, accum_out=st[:, 1:2]),
                waits=[(s_xl[b], 16 * nl[b]), (s_a, P.cnt[s_a])], inc=s_a)
            emit_ln_scalars()
            vw = [(s_d, P.cnt[s_d]), (s_a, P.cnt[s_a]),
                  (s_tp[b], state.setdefault("vlntp%d" % b, 0))]
            P.emit("scalar", lambda e, b=b: e.activation(
                out=vln[:, b, :], in_=xb[:, b, :], func=AF.Identity,
                scale=st[:, 7:8], bias=st[:, 8:9]),
                waits=vw, inc=s_a)
            state["lnapply%d" % b] = P.cnt[s_a]
            aval = P.cnt[s_a]
            for k in range(32):
                tw = [(s_a, aval)] if k == 0 else []
                if k == 0 and c == 0 and mmgate:
                    tw.append((s_mm, mmgate))
                P.emit("scalar", lambda e, b=b, c=c, k=k: e.dma_start(
                    out=xfm[:, k, c * 128:(c + 1) * 128],
                    in_=vln[:, b, k * 128:(k + 1) * 128], transpose=True),
                    waits=tw, inc=s_tp[b], incn=16)
            state["vlntp%d" % b] = P.cnt[s_tp[b]]
        tpw = [(s_tp[0], P.cnt[s_tp[0]]), (s_tp[1], P.cnt[s_tp[1]])]
        for pj in range(48):
            pb = pj % 2
            nwl[pb] += 1
            wsrc, col0 = (wq_d, pj * 128) if pj < 32 else \
                         ((wk_d, (pj - 32) * 128) if pj < 40 else (wv_d, (pj - 40) * 128))
            P.emit("sync", lambda e, wsrc=wsrc, col0=col0, pb=pb: e.dma_start(
                out=wts[:, pb, :, :],
                in_=wsrc.rearrange("(c p) h -> p c h", p=128)[:, :, col0:col0 + 128]),
                waits=[(s_mm, state.setdefault("wtsmm%d" % pb, 0))], inc=s_wl[pb], incn=16)
            pp = ppA if pb == 0 else ppB
            mmw = [(s_wl[pb], 16 * nwl[pb]),
                   (s_d, state.setdefault("ppread%d" % pb, 0)),
                   (s_a, state.setdefault("ppreada%d" % pb, 0))] + tpw
            for dc in range(32):
                P.emit("tensor", lambda e, pb=pb, dc=dc, pp=pp: e.matmul(
                    out=pp[:], lhsT=wts[:, pb, dc, :], rhs=xfm[:, dc, :],
                    start=(dc == 0), stop=(dc == 31)),
                    waits=mmw if dc == 0 else [], inc=s_mm if dc == 31 else None)
            state["wtsmm%d" % pb] = P.cnt[s_mm]
            mmv = P.cnt[s_mm]
            if pj < 40:
                P.emit("vector", lambda e, pp=pp, ts=ts: e.scalar_tensor_tensor(
                    out=ra[0:64, :], in0=pp[64:128, :], scalar=-1.0,
                    in1=sinb[0:64, ts * 512:(ts + 1) * 512], op0=ALU.mult, op1=ALU.mult),
                    waits=[(s_mm, mmv), (s_cs, 64), (s_d, P.cnt[s_d])], inc=s_d)
                P.emit("vector", lambda e, pp=pp, ts=ts: e.scalar_tensor_tensor(
                    out=ra[64:128, :], in0=pp[0:64, :], scalar=1.0,
                    in1=sinb[64:128, ts * 512:(ts + 1) * 512], op0=ALU.mult, op1=ALU.mult),
                    waits=[(s_d, P.cnt[s_d])], inc=s_d)
                P.emit("vector", lambda e, pp=pp, ts=ts: e.tensor_tensor(
                    out=rb[:], in0=pp[:], in1=cosb[:, ts * 512:(ts + 1) * 512],
                    op=ALU.mult),
                    waits=[(s_d, P.cnt[s_d])], inc=s_d)
                state["ppread%d" % pb] = P.cnt[s_d]
                if pj < 32:
                    qb = pj % 2
                    P.emit("vector", lambda e, qb=qb: e.tensor_tensor(
                        out=qrt[:, qb, :], in0=rb[:], in1=ra[:], op=ALU.add),
                        waits=[(s_d, P.cnt[s_d]),
                               (s_qst[qb], state.setdefault("qst%d" % qb, 0))], inc=s_d)
                    P.emit("gpsimd", lambda e, pj=pj, ts=ts, qb=qb: e.dma_start(
                        out=qr_d[pj, :, ts * 512:(ts + 1) * 512], in_=qrt[:, qb, :]),
                        waits=[(s_d, P.cnt[s_d])], inc=s_qst[qb], incn=16)
                    state["qst%d" % qb] = P.cnt[s_qst[qb]]
                else:
                    P.emit("vector", lambda e, pj=pj, ts=ts: e.tensor_tensor(
                        out=krF[:, pj - 32, ts * 512:(ts + 1) * 512],
                        in0=rb[:], in1=ra[:], op=ALU.add),
                        waits=[(s_d, P.cnt[s_d])], inc=s_d)
            else:
                P.emit("scalar", lambda e, pp=pp: e.activation(
                    out=vtmp[:], in_=pp[:], func=AF.Copy),
                    waits=[(s_mm, mmv), (s_a, P.cnt[s_a]),
                           (s_et, state.setdefault("vtp", 0))], inc=s_a)
                state["ppreada%d" % pb] = P.cnt[s_a]
                aval2 = P.cnt[s_a]
                for kt in range(4):
                    P.emit("scalar", lambda e, pj=pj, ts=ts, kt=kt: e.dma_start(
                        out=vTs[:, ts * 4 + kt, pj - 40, :],
                        in_=vtmp[:, kt * 128:(kt + 1) * 128], transpose=True),
                        waits=[(s_a, aval2)] if kt == 0 else [], inc=s_et, incn=16)
                state["vtp"] = P.cnt[s_et]
        state["xfm_mm"] = P.cnt[s_mm]

    P.sbfree("vtmp", "qrt", "rb", "ra", "wts", "xfm", "junka", "vln", "xb")
    phaseA_mm = P.cnt[s_mm]
    phaseA_d = P.cnt[s_d]
    phaseA_et = P.cnt[s_et]
    kvw = [(s_mm, phaseA_mm), (s_d, phaseA_d), (s_et, phaseA_et),
           (s_qst[0], P.cnt[s_qst[0]]), (s_qst[1], P.cnt[s_qst[1]])]

    Db8 = P.sb("Db8", [128, 8, T], BF16)
    E0 = P.sb("E0", [128, 16, 128], BF16)
    afm = P.sb("afm", [128, 32, 512], BF16)
    yofm = P.sb("yofm", [128, 32, 256], BF16)
    wots = P.sb("wots", [128, 2, 32, 128], BF16)
    ytm = P.sb("ytm", [128, D], BF16)
    x2t = P.sb("x2t", [128, D], F32)
    qrq = P.sb("qrq", [128, 1, 8, 128], BF16)
    lo8 = P.sb("lo8", [128, 8], F32)
    hi8 = P.sb("hi8", [128, 8], F32)
    mid8 = P.sb("mid8", [128, 8], F32)
    cn8 = P.sb("cn8", [128, 8], F32)
    geu = P.sb("geu", [128, 8], U32)
    ngu = P.sb("ngu", [128, 8], U32)
    tf8 = P.sb("tf8", [128, 8], F32)
    m8 = P.sb("m8", [128, 8], F32)
    z8 = P.sb("z8", [128, 8], F32)
    rz8 = P.sb("rz8", [128, 8], F32)

    c088 = P.constf(SCL)
    c40 = P.constf(-40.0)

    for qtg in range(16):
        qs, qtl = qtg // 4, qtg % 4
        W = (qtg + 1) * 128
        nkc = _ceil(W, 512)
        for hg in range(4):
            qb = (qtg * 4 + hg) % 2
            P.emit("sync", lambda e, hg=hg, qtg=qtg, qb=qb: e.dma_start(
                out=qrq[:, 0, :, :],
                in_=qr_d[hg * 8:(hg + 1) * 8, :, qtg * 128:(qtg + 1) * 128]
                .rearrange("h p q -> p h q")),
                waits=kvw + [(s_mm, state.setdefault("qrqmm", 0))],
                inc=s_ql[qb], incn=16)
            nql = P.cnt[s_ql[qb]]
            for hh in range(8):
                h = hg * 8 + hh
                hv = h // 4
                for kc in range(nkc):
                    cw = min(512, W - kc * 512)
                    pb2 = kc % 2
                    dp = dpA if pb2 == 0 else dpB
                    mmw = [(s_ql[qb], nql),
                           (s_a, state.setdefault("dbev%d" % pb2, 0))]
                    P.emit("tensor", lambda e, hh=hh, hv=hv, kc=kc, cw=cw, dp=dp, qb=qb: e.matmul(
                        out=dp[:, 0:cw], lhsT=qrq[:, 0, hh, :],
                        rhs=krF[:, hv, kc * 512:kc * 512 + cw],
                        start=True, stop=True),
                        waits=mmw, inc=s_mm)
                    P.emit("scalar", lambda e, hh=hh, kc=kc, cw=cw, dp=dp: e.activation(
                        out=Db8[:, hh, kc * 512:kc * 512 + cw], in_=dp[:, 0:cw],
                        func=AF.Copy),
                        waits=[(s_mm, P.cnt[s_mm]), (s_a, P.cnt[s_a]),
                               (s_d, state.setdefault("dbread", 0))], inc=s_a)
                    state["dbev%d" % pb2] = P.cnt[s_a]
                P.emit("vector", lambda e, hh=hh, qtg=qtg: e.tensor_tensor(
                    out=Db8[:, hh, qtg * 128:(qtg + 1) * 128],
                    in0=Db8[:, hh, qtg * 128:(qtg + 1) * 128], in1=trib[:], op=ALU.add),
                    waits=[(s_a, P.cnt[s_a]), (s_cs, 64), (s_d, P.cnt[s_d])], inc=s_d)
                P.emit("vector", lambda e, hh=hh, W=W: e.tensor_scalar(
                    out=junk[:, 0:W], in0=Db8[:, hh, 0:W], scalar1=0.0, scalar2=None,
                    op0=ALU.add, op1=ALU.max, accum_out=m8[:, hh:hh + 1]),
                    waits=[(s_d, P.cnt[s_d])], inc=s_d)
            state["qrqmm"] = P.cnt[s_mm]
            P.emit("vector", lambda e: e.tensor_scalar(
                out=lo8[:], in0=m8[:], scalar1=-256.0, scalar2=None, op0=ALU.add),
                waits=[(s_d, P.cnt[s_d])], inc=s_d)
            P.emit("vector", lambda e: e.tensor_scalar(
                out=hi8[:], in0=m8[:], scalar1=1.0, scalar2=None, op0=ALU.add),
                waits=[(s_d, P.cnt[s_d])], inc=s_d)
            for it in range(25):
                P.emit("vector", lambda e: e.tensor_tensor(
                    out=mid8[:], in0=lo8[:], in1=hi8[:], op=ALU.add),
                    waits=[(s_d, P.cnt[s_d])], inc=s_d)
                P.emit("vector", lambda e: e.tensor_scalar(
                    out=mid8[:], in0=mid8[:], scalar1=0.5, scalar2=None, op0=ALU.mult),
                    waits=[(s_d, P.cnt[s_d])], inc=s_d)
                for hh in range(8):
                    P.emit("vector", lambda e, hh=hh, W=W: e.tensor_scalar(
                        out=junk[:, 0:W], in0=Db8[:, hh, 0:W],
                        scalar1=mid8[:, hh:hh + 1], scalar2=None,
                        op0=ALU.is_ge, op1=ALU.add, accum_out=cn8[:, hh:hh + 1]),
                        waits=[(s_d, P.cnt[s_d])], inc=s_d)
                P.emit("vector", lambda e: e.tensor_scalar(
                    out=geu[:], in0=cn8[:], scalar1=float(NUM_REMAIN), scalar2=None,
                    op0=ALU.is_ge),
                    waits=[(s_d, P.cnt[s_d])], inc=s_d)
                P.emit("vector", lambda e: e.tensor_scalar(
                    out=ngu[:], in0=cn8[:], scalar1=float(NUM_REMAIN), scalar2=None,
                    op0=ALU.is_lt),
                    waits=[(s_d, P.cnt[s_d])], inc=s_d)
                P.emit("vector", lambda e: e.copy_predicated(
                    out=lo8[:], mask=geu[:], data=mid8[:]),
                    waits=[(s_d, P.cnt[s_d])], inc=s_d)
                P.emit("vector", lambda e: e.copy_predicated(
                    out=hi8[:], mask=ngu[:], data=mid8[:]),
                    waits=[(s_d, P.cnt[s_d])], inc=s_d)
            P.emit("vector", lambda e, qtg=qtg: e.scalar_tensor_tensor(
                out=tf8[:], in0=kab[:, qtg:qtg + 1].to_broadcast([128, 8]),
                scalar=-60256.0, in1=lo8[:], op0=ALU.mult, op1=ALU.add),
                waits=[(s_d, P.cnt[s_d]), (s_cs, 64)], inc=s_d)
            tsel = P.cnt[s_d]
            for hh in range(8):
                h = hg * 8 + hh
                hv = h // 4
                P.emit("vector", lambda e, hh=hh, W=W: e.tensor_scalar(
                    out=junk[:, 0:W], in0=Db8[:, hh, 0:W], scalar1=tf8[:, hh:hh + 1],
                    scalar2=-65536.0, op0=ALU.is_lt, op1=ALU.mult),
                    waits=[(s_d, tsel) if hh == 0 else (s_d, P.cnt[s_d])], inc=s_d)
                P.emit("vector", lambda e, hh=hh, W=W: e.tensor_tensor(
                    out=Db8[:, hh, 0:W], in0=Db8[:, hh, 0:W], in1=junk[:, 0:W],
                    op=ALU.add),
                    waits=[(s_d, P.cnt[s_d])], inc=s_d)
                P.emit("scalar", lambda e, hh=hh, W=W, c088=c088, c40=c40: e.activation(
                    out=Db8[:, hh, 0:W], in_=Db8[:, hh, 0:W], func=AF.Exp,
                    scale=c088, bias=c40, accum_out=z8[:, hh:hh + 1]),
                    waits=[(s_d, P.cnt[s_d]), (s_a, P.cnt[s_a]), P.cw()], inc=s_a)
            P.emit("vector", lambda e: e.reciprocal(out=rz8[:], in_=z8[:]),
                   waits=[(s_a, P.cnt[s_a]), (s_d, P.cnt[s_d])], inc=s_d)
            for hh in range(8):
                h = hg * 8 + hh
                hv = h // 4
                P.emit("vector", lambda e, hh=hh, W=W: e.tensor_scalar(
                    out=Db8[:, hh, 0:W], in0=Db8[:, hh, 0:W],
                    scalar1=rz8[:, hh:hh + 1], scalar2=None, op0=ALU.mult),
                    waits=[(s_d, P.cnt[s_d]), (s_mm, state.setdefault("etread", 0))],
                    inc=s_d)
                nval = P.cnt[s_d]
                nkt = W // 128
                for kt in range(nkt):
                    P.emit("scalar", lambda e, hh=hh, kt=kt: e.dma_start(
                        out=E0[:, kt, :], in_=Db8[:, hh, kt * 128:(kt + 1) * 128],
                        transpose=True),
                        waits=[(s_d, nval), (s_mm, state.setdefault("avmm", 0))]
                        if kt == 0 else [], inc=s_et, incn=16)
                etv = P.cnt[s_et]
                av = avA if hh % 2 == 0 else avB
                for kt in range(nkt):
                    P.emit("tensor", lambda e, kt=kt, hv=hv, av=av, nkt=nkt: e.matmul(
                        out=av[:], lhsT=vTs[:, kt, hv, :], rhs=E0[:, kt, :],
                        start=(kt == 0), stop=(kt == nkt - 1)),
                        waits=[(s_et, etv), (s_a, state.setdefault("avev%d" % (hh % 2), 0))]
                        if kt == 0 else [], inc=s_mm if kt == nkt - 1 else None)
                state["etread"] = P.cnt[s_mm]
                state["avmm"] = P.cnt[s_mm]
                P.emit("scalar", lambda e, h=h, qtl=qtl, av=av: e.activation(
                    out=afm[:, h, qtl * 128:(qtl + 1) * 128], in_=av[:], func=AF.Copy),
                    waits=[(s_mm, P.cnt[s_mm]), (s_a, P.cnt[s_a]),
                           (s_mm, state.setdefault("afmread", 0))], inc=s_a)
                state["avev%d" % (hh % 2)] = P.cnt[s_a]
            state["dbread"] = P.cnt[s_d]
        if qtl == 3:
            afmdone = P.cnt[s_a]
            for oh in range(2):
                q0 = oh * 256
                for dt in range(32):
                    pb = dt % 2
                    P.emit("sync", lambda e, dt=dt, pb=pb: e.dma_start(
                        out=wots[:, pb, :, :],
                        in_=wo_d.rearrange("(c p) h -> p c h", p=128)[:, :, dt * 128:(dt + 1) * 128]),
                        waits=[(s_mm, state.setdefault("womm%d" % pb, 0))],
                        inc=s_wl[pb], incn=16)
                    op = opA if pb == 0 else opB
                    mmw = [(s_wl[pb], P.cnt[s_wl[pb]]), (s_a, afmdone),
                           (s_a, state.setdefault("opev%d" % pb, 0))]
                    for hc in range(32):
                        P.emit("tensor", lambda e, dt=dt, hc=hc, op=op, q0=q0: e.matmul(
                            out=op[:], lhsT=wots[:, dt % 2, hc, :],
                            rhs=afm[:, hc, q0:q0 + 256],
                            start=(hc == 0), stop=(hc == 31)),
                            waits=mmw if hc == 0 else [],
                            inc=s_mm if hc == 31 else None)
                    state["womm%d" % pb] = P.cnt[s_mm]
                    P.emit("scalar", lambda e, dt=dt, pb=pb, op=op: e.activation(
                        out=yofm[:, dt, :], in_=op[:], func=AF.Copy),
                        waits=[(s_mm, P.cnt[s_mm]), (s_a, P.cnt[s_a]),
                               (s_ytp, state.setdefault("ytpd", 0))], inc=s_a)
                    state["opev%d" % pb] = P.cnt[s_a]
                aval3 = P.cnt[s_a]
                for c2 in range(2):
                    tc = qs * 4 + oh * 2 + c2
                    for k in range(32):
                        tw = [(s_a, max(aval3, state.setdefault("x2apply", 0))),
                              (s_d, state.setdefault("plnd", 0))] if k == 0 else []
                        P.emit("scalar", lambda e, c2=c2, k=k: e.dma_start(
                            out=ytm[:, k * 128:(k + 1) * 128],
                            in_=yofm[:, k, c2 * 128:(c2 + 1) * 128], transpose=True),
                            waits=tw, inc=s_ytp, incn=16)
                    state["ytpd"] = P.cnt[s_ytp]
                    P.emit("vector", lambda e: e.tensor_scalar(
                        out=junk[:], in0=ytm[:], scalar1=0.0, scalar2=None,
                        op0=ALU.add, op1=ALU.add, accum_out=st[:, 0:1]),
                        waits=[(s_ytp, P.cnt[s_ytp]), (s_d, P.cnt[s_d])], inc=s_d)
                    P.emit("scalar", lambda e: e.activation(
                        out=x2t[:], in_=ytm[:], func=AF.Square, accum_out=st[:, 1:2]),
                        waits=[(s_ytp, P.cnt[s_ytp]), (s_a, P.cnt[s_a]),
                               (s_out, state.setdefault("acc_out", 0))], inc=s_a)
                    emit_ln_scalars()
                    P.emit("scalar", lambda e: e.activation(
                        out=x2t[:], in_=ytm[:], func=AF.Identity,
                        scale=st[:, 7:8], bias=st[:, 8:9]),
                        waits=[(s_d, P.cnt[s_d]), (s_a, P.cnt[s_a]),
                               (s_out, state.setdefault("acc_out", 0))], inc=s_a)
                    state["x2apply"] = P.cnt[s_a]
                    state["plnd"] = P.cnt[s_d]
                    for hh2 in range(2):
                        P.emit("gpsimd", lambda e, tc=tc, hh2=hh2: e.dma_start(
                            out=x3_d[tc * 128:(tc + 1) * 128, hh2 * 2048:(hh2 + 1) * 2048],
                            in_=x2t[:, hh2 * 2048:(hh2 + 1) * 2048], accum_op=ALU.add),
                            waits=[(s_a, P.cnt[s_a]), (s_xc, 16 * 16)] if hh2 == 0 else [],
                            inc=s_out, incn=16)
                    state["acc_out"] = P.cnt[s_out]
            state["afmread"] = P.cnt[s_mm]

    P.emit("sync", None, waits=[(s_out, P.cnt[s_out])])
    return P.build()


TRI_CONST = np.where(np.arange(128)[:, None] < np.arange(128)[None, :],
                     -57344.0, 0.0).astype(BF)
KEEPALL = np.zeros((128, 16), np.float32)
for _qt in range(16):
    KEEPALL[:, _qt] = (_qt * 128 + np.arange(128) + 1 <= NUM_REMAIN)
EMAT = np.zeros((128, 128), np.float32)
for _hr in range(128):
    EMAT[_hr // 16, _hr] = 16.0
EMAT = EMAT.astype(BF)


def kernel(**inputs):
    prep = run_prep(inputs)
    x = np.asarray(inputs["L0_attn_x"])
    nc = build_l0a()
    im = [{"x": np.ascontiguousarray(x[c]), "Cb": prep["C"], "Sb": prep["S"],
           "C2b": prep["C2"]} for c in range(N_CORES)]
    r = run_bass_kernel_spmd(nc, im, core_ids=list(range(N_CORES)))
    x1 = [np.asarray(r.results[c]["x1"]) for c in range(N_CORES)]
    nc = build_l0f()
    im = [{"x1": x1[c], "w1t": prep["w1t"], "w2t": prep["w2t"],
           "w3t": prep["w3t"]} for c in range(N_CORES)]
    r = run_bass_kernel_spmd(nc, im, core_ids=list(range(N_CORES)))
    x2 = [np.asarray(r.results[c]["x2"]) for c in range(N_CORES)]
    nc = build_l1a()
    im = [{"x2": x2[c], "wqT": prep["wqT"], "wkT": prep["wkT"], "wvT": prep["wvT"],
           "woT": prep["woT"], "cosF": prep["cosF"], "sinF": prep["sinF"],
           "tri": TRI_CONST, "keepall": KEEPALL} for c in range(N_CORES)]
    r = run_bass_kernel_spmd(nc, im, core_ids=list(range(N_CORES)))
    x3 = [np.asarray(r.results[c]["x3"]) for c in range(N_CORES)]
    wrtp = np.zeros((D, 128), dtype=BF)
    wrtp[:, 0:8] = prep["WrT"]
    brb = np.ascontiguousarray(np.broadcast_to(
        np.asarray(inputs["L1_ffn_br"], dtype=np.float32), (128, 8)))
    nc = build_l1f()
    im = [{"x3": x3[c], "att": prep["AT"], "bst": prep["BsT"], "wrt": wrtp,
           "brb": brb, "emat": EMAT} for c in range(N_CORES)]
    r = run_bass_kernel_spmd(nc, im, core_ids=list(range(N_CORES)))
    return np.stack([np.asarray(r.results[c]["xo"]) for c in range(N_CORES)])
